# version 01
# baseline (speedup 1.0000x reference)
import numpy as np
import ml_dtypes
import concourse.bass as bass
import concourse.mybir as mybir

F32 = mybir.dt.float32
BF16 = mybir.dt.bfloat16
AF = mybir.ActivationFunctionType
ALU = mybir.AluOpType
AX = mybir.AxisListType

T = 1024
NCH = T // 128
E = 4096
NE = E // 128
HQ, KVH, D, C = 32, 8, 128, 128
LN_EPS = 1e-5
RMS_EPS = 1e-5

WSL_W = 16384
ABUF_W = 11264
FBUF_W = 4160


class Sem:
    def __init__(self, nc, name):
        self.h = nc.semaphore(name).__enter__()
        self.v = 0

    def inc(self, instr, k=1):
        instr.then_inc(self.h, k)
        self.v += k
        return instr


SEM_ALIAS = {
    "as_ev": "cp0",
    "as_ex": "cp1",
    "as_ld": "swp0",
    "as_mm": "cp2",
    "as_sf": "cp3",
    "as_t1": "cp4",
    "as_t2": "cp5",
    "as_t3": "cp6",
    "as_t4": "cp7",
    "as_vv": "cp8",
    "aw_mm": "cp0",
    "aw_va": "cp1",
    "aw_wl": "hwp0",
    "aw_yl": "swp0",
    "bs_b2": "cp0",
    "bs_bl": "swp0",
    "bs_bw": "hwp0",
    "bs_ld": "swp1",
    "bs_vv": "cp1",
    "bs_wr": "hwp1",
    "cr": "swp0",
    "cw": "hwp0",
    "fe": "cp0",
    "ffl0": "swp0",
    "fft1": "cp1",
    "fft2": "cp2",
    "fl": "swp1",
    "fm": "cp3",
    "fmm": "cp0",
    "fo": "hwp0",
    "fq": "cp4",
    "fs": "cp5",
    "fsqf": "cp6",
    "fu": "cp7",
    "fva": "cp1",
    "fvf": "cp8",
    "fwld": "hwp0",
    "ldc": "swp0",
    "ldx": "swp0",
    "lht": "hwp0",
    "lhw": "hwp1",
    "llnh": "cp0",
    "lnrm": "cp1",
    "lrcp": "cp2",
    "lred": "cp3",
    "lsf": "cp4",
    "lstv": "cp5",
    "ocv": "cp0",
    "ow": "hwp0",
    "tr": "hwp0",
    "tw": "hwp1",
}


class Ctx:
    def __init__(self, nc):
        self.nc = nc
        self.d = {}
        self.s = {}
        self.p = {}
        self._semn = 0

    def dram(self, name, shape, dtype, kind="Internal"):
        t = self.nc.dram_tensor(name, shape, dtype, kind=kind)
        self.d[name] = t
        return t

    def sbuf(self, name, shape, dtype):
        t = self.nc.sbuf_tensor(name, shape, dtype).__enter__()
        self.s[name] = t
        return t

    def psum(self, name, shape, dtype=F32):
        t = self.nc.psum_tensor(name, shape, dtype).__enter__()
        self.p[name] = t
        return t

    def sem(self, tag="", cached=True):
        tag = SEM_ALIAS.get(tag, tag)
        if cached:
            if not hasattr(self, "_sems"):
                self._sems = {}
            if tag in self._sems:
                return self._sems[tag]
        self._semn += 1
        s = Sem(self.nc, f"s{self._semn}_{tag}")
        if not hasattr(self, "all_sems"):
            self.all_sems = []
        self.all_sems.append(s)
        if cached:
            self._sems[tag] = s
        return s



def host_consts():
    con = {}
    con["c_ident_bf"] = np.eye(128).astype(ml_dtypes.bfloat16)
    con["c_ident_f32"] = np.eye(128).astype(np.float32)
    j = np.arange(128)[:, None]; i = np.arange(128)[None, :]
    con["c_mask"] = np.where(i >= j, 0.0, -1e30).astype(np.float32)
    con["c_ones_col_bf"] = np.ones((128, 16), ml_dtypes.bfloat16)
    con["c_ones4"] = np.ones((4, 2048), ml_dtypes.bfloat16)
    return con


def rope_host_tables():
    inv_freq = 1.0 / (10000.0 ** (np.arange(0, D, 2, dtype=np.float64) / D))
    ang = np.arange(2048, dtype=np.float64)[:, None] * inv_freq
    cos = np.concatenate([np.cos(ang), np.cos(ang)], axis=-1)
    sin = np.concatenate([np.sin(ang), np.sin(ang)], axis=-1)
    s2 = sin.copy()
    s2[:, :64] *= -1.0
    return cos.astype(np.float32), s2.astype(np.float32)


WEIGHT_NAMES = [
    ("aWq", [4096, 4096]), ("aWk", [1024, 4096]), ("aWv", [1024, 4096]),
    ("aWo", [4096, 4096]), ("aWg", [4096, 4096]), ("aWin", [32, 4096]),
    ("a_dtb", [32]), ("a_Alog", [32]), ("a_gnorm", [128]),
    ("bWq", [4096, 128, 32]), ("b_bq", [128, 32]),
    ("bWk", [4096, 128, 8]), ("b_bk", [128, 8]),
    ("bWv", [4096, 128, 8]), ("b_bv", [128, 8]),
    ("fW", [4096, 4096]), ("f_lA", [32, 4096]), ("f_lB", [4096, 32]),
    ("f_mag", [4096]),
]

CONST_SHAPES = {
    "c_ident_bf": ([128, 128], BF16), "c_ident_f32": ([128, 128], F32),
    "c_mask": ([128, 128], F32),
    "c_ones_col_bf": ([128, 16], BF16), "c_ones4": ([4, 2048], BF16),
}

W_NAT_ROWS = {"wq": 4096, "wk": 1024, "wv": 1024, "wo": 4096, "wg": 4096, "fw": 4096}
W_SRC = {"wq": "aWq", "wk": "aWk", "wv": "aWv", "wo": "aWo", "wg": "aWg"}

FB_MAG = 0
FB_SC = 32
FB_LAB = 64
FB_STRIP = FB_LAB
FA_STATS = FB_LAB
FA_STATST = FB_LAB + 1024
FA_EXPT = FB_LAB + 1792
FA_COS = FB_LAB + 2560
FA_ARG = FB_LAB + 3072
FA_Y = FB_LAB + 3328
FA_STATE = FB_LAB + 3584
FB_OUT = FB_LAB

AB_K = 0
AB_KT = 2048
AB_VT = 4096
AB_SD = 6144


def declare_io(ctx, outputs=("out",)):
    ctx.dram("x0", [T, E], F32, kind="ExternalInput")
    ctx.dram("xprev", [T, E], F32, kind="ExternalInput")
    for nm in ["cosT_own", "s2T_own", "cosT_prev", "s2T_prev"]:
        ctx.dram(nm, [128, T], F32, kind="ExternalInput")
    for nm, shp in WEIGHT_NAMES:
        ctx.dram(nm, shp, F32, kind="ExternalInput")
    for nm, (shp, dt) in CONST_SHAPES.items():
        ctx.dram(nm, shp, dt, kind="ExternalInput")
    for o in outputs:
        ctx.dram(o, [T, E], F32, kind="ExternalOutput")
    ctx.dram("h_hbm", [T, E], BF16)
    ctx.dram("b_hbm", [T, E], BF16)
    ctx.dram("carry_hbm", [HQ * 128, 128], F32)
    ctx.dram("yg_hbm", [HQ * 128, T], BF16)
    ctx.dram("st_hbm", [128, 8192], BF16)
    ctx.dram("brow_hbm", [1, 3072], BF16)
    for nm, rows in W_NAT_ROWS.items():
        ctx.dram(nm + "_nat", [rows, E], BF16)
        ctx.dram(nm + "_t", [E, rows], BF16)
    ctx.dram("bqsum_nat", [E, 1024], BF16)
    ctx.dram("bk8_nat", [E, 1024], BF16)
    ctx.dram("bv8_nat", [E, 1024], BF16)


def declare_sbuf(ctx):
    ctx.sbuf("x_sb", [128, NCH * E], BF16)
    ctx.sbuf("h_fm", [128, NE * T], BF16)
    ctx.sbuf("wsl", [128, WSL_W], BF16)
    ctx.sbuf("abuf", [128, ABUF_W], BF16)
    ctx.sbuf("fbuf", [128, FBUF_W], F32)
    ctx.sbuf("consts_bf", [128, 144], BF16)
    ctx.sbuf("consts_f", [128, 256], F32)
    ctx.sbuf("small_f", [128, 64], F32)
    ctx.psum("pA", [128, 512]); ctx.psum("pB", [128, 512])
    ctx.psum("pC", [128, 512]); ctx.psum("pD", [128, 512])
    ctx.psum("pY", [128, 512]); ctx.psum("pZ", [128, 512])
    ctx.psum("pT1", [128, 1024], BF16); ctx.psum("pT2", [128, 1024], BF16)


def cident_bf(ctx, n=128):
    return ctx.s["consts_bf"][0:n, 0:n]


def cident_f(ctx, n=128):
    return ctx.s["consts_f"][0:n, 0:n]


def cmask(ctx):
    return ctx.s["consts_f"][:, 128:256]



def emit_load_consts(ctx):
    nc = ctx.nc
    s, d = ctx.s, ctx.d
    sm = ctx.sem("ldc")
    with nc.Block() as blk:
        @blk.gpsimd
        def _(g):
            g.memset(s["small_f"][:, 8:9], 1.0)
            g.memset(s["small_f"][:, 9:10], RMS_EPS)
            g.memset(s["small_f"][:, 10:11], -0.5)
            g.memset(s["small_f"][:, 11:12], -1.0)
            sm.inc(g.dma_start(s["consts_bf"][:, 0:128], d["c_ident_bf"][:, :]), 16)
            sm.inc(g.dma_start(s["consts_bf"][:, 128:144], d["c_ones_col_bf"][:, :]), 16)
            sm.inc(g.dma_start(s["consts_f"][:, 0:128], d["c_ident_f32"][:, :]), 16)
            sm.inc(g.dma_start(s["consts_f"][:, 128:256], d["c_mask"][:, :]), 16)
            sm.inc(g.dma_start(s["small_f"][0:32, 0:1], d["a_dtb"][:].rearrange("(a b) -> a b", b=1)), 16)
            sm.inc(g.dma_start(s["small_f"][0:32, 1:2], d["a_Alog"][:].rearrange("(a b) -> a b", b=1)), 16)
            sm.inc(g.dma_start(s["small_f"][:, 3:4], d["a_gnorm"][:].rearrange("(a b) -> a b", b=1)), 16)
            for i in range(32):
                sm.inc(g.dma_start(s["fbuf"][:, FB_MAG + i:FB_MAG + i + 1],
                                   d["f_mag"][i * 128:(i + 1) * 128].rearrange(
                                       "(a b) -> a b", b=1)), 16)
            g.wait_ge(sm.h, sm.v)
    with nc.Block() as blk:
        @blk.scalar
        def _(sc):
            sc.activation(s["small_f"][0:32, 2:3], s["small_f"][0:32, 1:2], AF.Exp)
    with nc.Block() as blk:
        @blk.vector
        def _(v):
            v.tensor_scalar(s["small_f"][0:32, 2:3], s["small_f"][0:32, 2:3],
                            -1.0, None, ALU.mult)



def emit_scratch_cast(ctx, src2d, dst_nat, rows):
    nc = ctx.nc
    s = ctx.s
    nblk = rows // 128
    smr = ctx.sem("cr"); b_r = smr.v
    smw = ctx.sem("cw"); b_w = smw.v
    with nc.Block() as blk:
        @blk.gpsimd
        def _(g):
            for i in range(nblk):
                par = i % 2
                if i >= 2:
                    g.wait_ge(smw.h, b_w + 16 * (i - 1))
                smr.inc(g.dma_start(s["wsl"][:, par * 8192: par * 8192 + 4096],
                                    src2d[i * 128:(i + 1) * 128, :]), 16)
        @blk.sync
        def _(sy):
            for i in range(nblk):
                par = i % 2
                sy.wait_ge(smr.h, b_r + 16 * (i + 1))
                smw.inc(sy.dma_start(ctx.d[dst_nat][i * 128:(i + 1) * 128, :],
                                     s["wsl"][:, par * 8192: par * 8192 + 4096]), 16)
            sy.wait_ge(smw.h, smw.v)


def emit_scratch_transpose(ctx, src_nat, dst_t, rows):
    nc = ctx.nc
    s = ctx.s
    smr = ctx.sem("tr"); b_tw = None
    smw = ctx.sem("tw"); b_tw = smw.v
    with nc.Block() as blk:
        @blk.sync
        def _(sy):
            for i in range(NE):
                par = i % 2
                buf = s["wsl"][:, par * 8192: par * 8192 + rows]
                if i >= 2:
                    sy.wait_ge(smw.h, b_tw + 16 * (i - 1))
                smr.inc(sy.dma_start(buf, ctx.d[src_nat][:, i * 128:(i + 1) * 128],
                                     transpose=True), 16)
                sy.wait_ge(smr.h, smr.v)
                smw.inc(sy.dma_start(ctx.d[dst_t][i * 128:(i + 1) * 128, :], buf), 16)
            sy.wait_ge(smw.h, smw.v)


def emit_scratch_ffn(ctx):
    nc = ctx.nc
    s, d, p = ctx.s, ctx.d, ctx.p
    WLA = 8192
    WLB = 12288
    ld0 = ctx.sem("ffl0"); b_l0 = ld0.v
    fft1 = ctx.sem("fft1"); b_f1 = fft1.v
    fft2 = ctx.sem("fft2"); b_f2 = fft2.v
    with nc.Block() as blk:
        @blk.gpsimd
        def _(g):
            ld0.inc(g.dma_start(s["wsl"][0:32, WLA:WLA + 4096], d["f_lA"][:, :]), 16)
            for i in range(32):
                if i >= 1:
                    g.wait_ge(fft1.h, b_f1 + i)
                ld0.inc(g.dma_start(s["fbuf"][:, FB_STRIP:FB_STRIP + 32],
                                    d["f_lB"][i * 128:(i + 1) * 128, :]), 16)
        @blk.tensor
        def _(t):
            for i in range(32):
                t.wait_ge(ld0.h, b_l0 + 16 * (i + 2))
                if i >= 2:
                    t.wait_ge(fft2.h, b_f2 + i - 1)
                fft1.inc(t.transpose((p["pC"] if i % 2 == 0 else p["pD"])[0:32, 0:128],
                                     s["fbuf"][:, FB_STRIP:FB_STRIP + 32],
                                     cident_f(ctx)), 1)
        @blk.vector
        def _(v):
            for i in range(32):
                v.wait_ge(fft1.h, b_f1 + i + 1)
                fft2.inc(v.tensor_copy(
                    s["wsl"][0:32, WLB + i * 128: WLB + (i + 1) * 128],
                    (p["pC"] if i % 2 == 0 else p["pD"])[0:32, 0:128]), 1)

    for pass2 in (False, True):
        ld = ctx.sem("fl"); b_ld = ld.v
        mm = ctx.sem("fm"); b_mm = mm.v
        ev = ctx.sem("fe"); b_ev = ev.v
        sq = ctx.sem("fq"); b_sq = sq.v
        sc = ctx.sem("fs"); b_sc = sc.v
        su = ctx.sem("fu"); b_su = su.v
        vf = ctx.sem("fvf"); b_vf = vf.v
        sqf = ctx.sem("fsqf"); b_sf = sqf.v
        smo = ctx.sem("fo"); b_smo = smo.v
        with nc.Block() as blk:
            @blk.gpsimd
            def _(g):
                for i in range(32):
                    for jj in range(8):
                        n = i * 8 + jj
                        if n >= 2:
                            g.wait_ge(sq.h, b_sq + n - 1)
                        ld.inc(g.dma_start(
                            s["fbuf"][:, FB_STRIP + (n % 2) * 512:FB_STRIP + (n % 2) * 512 + 512],
                            d["fW"][i * 128:(i + 1) * 128, jj * 512:(jj + 1) * 512]), 16)
            @blk.tensor
            def _(t):
                for i in range(32):
                    for jj in range(8):
                        n = i * 8 + jj
                        pp = p["pA"] if n % 2 == 0 else p["pB"]
                        if n >= 2:
                            t.wait_ge(ev.h, b_ev + n - 1)
                        mm.inc(t.matmul(pp[:, :],
                                        s["wsl"][0:32, WLB + i * 128: WLB + (i + 1) * 128],
                                        s["wsl"][0:32, WLA + jj * 512: WLA + (jj + 1) * 512],
                                        start=True, stop=True), 1)
            @blk.vector
            def _(v):
                for i in range(32):
                    for jj in range(8):
                        n = i * 8 + jj
                        pp = p["pA"] if n % 2 == 0 else p["pB"]
                        v.wait_ge(mm.h, b_mm + n + 1)
                        v.wait_ge(ld.h, b_ld + 16 * (n + 1))
                        st = FB_STRIP + (n % 2) * 512
                        ev.inc(v.tensor_tensor(s["fbuf"][:, st:st + 512], pp[:, :],
                                               s["fbuf"][:, st:st + 512], ALU.add), 1)
                    if not pass2:
                        v.wait_ge(sq.h, b_sq + 8 * (i + 1))
                        vf.inc(v.reduce_sum(s["fbuf"][:, 34:35], s["fbuf"][:, 36:44],
                                            axis=AX.X), 1)
                        v.wait_ge(vf.h, b_vf + i + 1)
                        if i >= 1:
                            v.wait_ge(su.h, b_su + i)
                        sc.inc(v.reciprocal(s["fbuf"][:, 35:36], s["fbuf"][:, 34:35]), 1)
            if not pass2:
                @blk.scalar
                def _(scl):
                    for i in range(32):
                        for jj in range(8):
                            n = i * 8 + jj
                            scl.wait_ge(ev.h, b_ev + n + 1)
                            st = FB_STRIP + (n % 2) * 512
                            sq.inc(scl.activation(s["wsl"][:, 0:512],
                                                  s["fbuf"][:, st:st + 512], AF.Square,
                                                  accum_out=s["fbuf"][:, 36 + jj:37 + jj]), 1)
                        scl.wait_ge(sc.h, b_sc + i + 1)
                        sqf.inc(scl.activation(s["fbuf"][:, 35:36], s["fbuf"][:, 35:36],
                                               AF.Sqrt), 1)
                        scl.wait_ge(sqf.h, b_sf + i + 1)
                        su.inc(scl.mul(s["small_f"][:, 16 + i:17 + i], s["fbuf"][:, 35:36],
                               s["fbuf"][:, FB_MAG + i:FB_MAG + i + 1]), 1)
            else:
                @blk.scalar
                def _(scl):
                    for i in range(32):
                        for jj in range(8):
                            n = i * 8 + jj
                            scl.wait_ge(ev.h, b_ev + n + 1)
                            if n >= 4:
                                scl.wait_ge(smo.h, b_smo + 16 * (n - 3))
                            st = FB_STRIP + (n % 2) * 512
                            sq.inc(scl.activation(
                                s["wsl"][:, (n % 4) * 512:(n % 4 + 1) * 512],
                                s["fbuf"][:, st:st + 512], AF.Copy,
                                scale=s["small_f"][:, 16 + i:17 + i]), 1)
                @blk.sync
                def _(sy):
                    for i in range(32):
                        for jj in range(8):
                            n = i * 8 + jj
                            sy.wait_ge(sq.h, b_sq + n + 1)
                            smo.inc(sy.dma_start(
                                d["fw_nat"][i * 128:(i + 1) * 128, jj * 512:(jj + 1) * 512],
                                s["wsl"][:, (n % 4) * 512:(n % 4 + 1) * 512]), 16)
                    sy.wait_ge(smo.h, smo.v)



def emit_ln(ctx):
    nc = ctx.nc
    s, d, p = ctx.s, ctx.d, ctx.p
    LNC = 48
    red = ctx.sem("lred"); b_red = red.v
    rcp = ctx.sem("lrcp"); b_rcp = rcp.v
    stv = ctx.sem("lstv"); b_stv = stv.v
    nrm2 = ctx.sem("lnrm"); b_nrm = nrm2.v
    lnh = ctx.sem("llnh"); b_lnh = lnh.v
    hw = ctx.sem("lhw"); b_hw = hw.v
    ht = ctx.sem("lht")
    sf = ctx.sem("lsf"); b_sf = sf.v
    fb = s["fbuf"]
    with nc.Block() as blk:
        @blk.vector
        def _(v):
            for ch in range(NCH):
                xc = s["x_sb"][:, ch * E:(ch + 1) * E]
                red.inc(v.reduce_sum(fb[:, LNC:LNC + 1], xc, axis=AX.X), 1)
                v.wait_ge(rcp.h, b_rcp + ch + 1)
                stv.inc(v.reciprocal(fb[:, LNC + 5:LNC + 6], fb[:, LNC + 3:LNC + 4]), 1)
                v.wait_ge(nrm2.h, b_nrm + ch + 1)
                if ch >= 1:
                    v.wait_ge(hw.h, b_hw + 16 * ch)
                lnh.inc(v.scalar_tensor_tensor(
                    s["wsl"][:, E:2 * E], xc, fb[:, LNC + 6:LNC + 7],
                    bass.AP(fb, LNC + 7, [[FBUF_W, 128], [0, E]]),
                    ALU.mult, ALU.add), 1)
        @blk.scalar
        def _(sc):
            for ch in range(NCH):
                xc = s["x_sb"][:, ch * E:(ch + 1) * E]
                sc.wait_ge(red.h, b_red + ch + 1)
                sc.activation(s["wsl"][:, 0:E], xc, AF.Square,
                              accum_out=fb[:, LNC + 4:LNC + 5])
                sf.inc(sc.mul(fb[:, LNC + 1:LNC + 2], fb[:, LNC:LNC + 1], 1.0 / E), 1)
                sc.wait_ge(sf.h, b_sf + 5 * ch + 1)
                sf.inc(sc.mul(fb[:, LNC + 2:LNC + 3], fb[:, LNC + 1:LNC + 2],
                              fb[:, LNC + 1:LNC + 2]), 1)
                sc.wait_ge(sf.h, b_sf + 5 * ch + 2)
                sf.inc(sc.activation(fb[:, LNC + 2:LNC + 3], fb[:, LNC + 2:LNC + 3],
                                     AF.Copy, bias=LN_EPS, scale=-1.0), 1)
                sc.mul(fb[:, LNC + 3:LNC + 4], fb[:, LNC + 4:LNC + 5], 1.0 / E)
                sc.wait_ge(sf.h, b_sf + 5 * ch + 3)
                rcp.inc(sc.add(fb[:, LNC + 3:LNC + 4], fb[:, LNC + 3:LNC + 4],
                               fb[:, LNC + 2:LNC + 3]), 1)
                sc.wait_ge(stv.h, b_stv + ch + 1)
                sf.inc(sc.activation(fb[:, LNC + 6:LNC + 7], fb[:, LNC + 5:LNC + 6],
                                     AF.Sqrt), 1)
                sc.wait_ge(sf.h, b_sf + 5 * ch + 4)
                sf.inc(sc.mul(fb[:, LNC + 7:LNC + 8], fb[:, LNC + 1:LNC + 2],
                              fb[:, LNC + 6:LNC + 7]), 1)
                sc.wait_ge(sf.h, b_sf + 5 * ch + 5)
                nrm2.inc(sc.mul(fb[:, LNC + 7:LNC + 8], fb[:, LNC + 7:LNC + 8], -1.0), 1)
        @blk.sync
        def _(sy):
            for ch in range(NCH):
                sy.wait_ge(lnh.h, b_lnh + ch + 1)
                hw.inc(sy.dma_start(d["h_hbm"][ch * 128:(ch + 1) * 128, :],
                                    s["wsl"][:, E:2 * E]), 16)
            sy.wait_ge(hw.h, hw.v)
    with nc.Block() as blk:
        @blk.sync
        def _(sy):
            for k in range(NE):
                ht.inc(sy.dma_start(s["h_fm"][:, k * T:(k + 1) * T],
                                    d["h_hbm"][:, k * 128:(k + 1) * 128],
                                    transpose=True), 16)
            sy.wait_ge(ht.h, ht.v)


def emit_ffn_main(ctx):
    nc = ctx.nc
    s, d, p = ctx.s, ctx.d, ctx.p
    wld = ctx.sem("fwld"); b_wld = wld.v
    mmv = ctx.sem("fmm"); b_mm = mmv.v
    vad = ctx.sem("fva"); b_va = vad.v
    with nc.Block() as blk:
        @blk.sync
        def _(sy):
            for ot in range(16):
                par = ot % 2
                if ot >= 2:
                    sy.wait_ge(mmv.h, b_mm + (ot - 1) * NCH)
                wld.inc(sy.dma_start(
                    s["wsl"][:, par * 8192: par * 8192 + NE * 256].rearrange(
                        "p (k j) -> p k j", k=NE),
                    d["fw_t"][:, ot * 256:(ot + 1) * 256].rearrange(
                        "(k p) j -> p k j", p=128)), 16)
        @blk.tensor
        def _(t):
            for ot in range(16):
                par = ot % 2
                t.wait_ge(wld.h, b_wld + 16 * (ot + 1))
                for ch in range(NCH):
                    n = ot * NCH + ch
                    pp = p["pA"] if n % 2 == 0 else p["pB"]
                    if n >= 2:
                        t.wait_ge(vad.h, b_va + n - 1)
                    for k in range(NE):
                        ins = t.matmul(
                            pp[:, 0:256],
                            s["h_fm"][:, k * T + ch * 128: k * T + (ch + 1) * 128],
                            s["wsl"][:, par * 8192 + k * 256: par * 8192 + (k + 1) * 256],
                            start=(k == 0), stop=(k == NE - 1))
                        if k == NE - 1:
                            mmv.inc(ins, 1)
        @blk.vector
        def _(v):
            for ot in range(16):
                for ch in range(NCH):
                    n = ot * NCH + ch
                    pp = p["pA"] if n % 2 == 0 else p["pB"]
                    v.wait_ge(mmv.h, b_mm + n + 1)
                    xsl = s["x_sb"][:, ch * E + ot * 256: ch * E + (ot + 1) * 256]
                    vad.inc(v.tensor_tensor(xsl, xsl, pp[:, 0:256], ALU.add), 1)



def emit_load_x(ctx, src="x0"):
    nc = ctx.nc
    s, d = ctx.s, ctx.d
    sm = ctx.sem("ldx")
    with nc.Block() as blk:
        @blk.gpsimd
        def _(g):
            for ch in range(NCH):
                sm.inc(g.dma_start(s["x_sb"][:, ch * E:(ch + 1) * E],
                                   d[src][ch * 128:(ch + 1) * 128, :]), 16)
            g.wait_ge(sm.h, sm.v)


def emit_store_out(ctx, dst="out"):
    nc = ctx.nc
    s, d = ctx.s, ctx.d
    cv = ctx.sem("ocv"); b_cv = cv.v
    ow = ctx.sem("ow"); b_ow = ow.v
    with nc.Block() as blk:
        @blk.vector
        def _(v):
            for ch in range(NCH):
                if ch >= 1:
                    v.wait_ge(ow.h, b_ow + 16 * ch)
                cv.inc(v.tensor_copy(s["fbuf"][:, FB_OUT:FB_OUT + E],
                                     s["x_sb"][:, ch * E:(ch + 1) * E]), 1)
        @blk.sync
        def _(sy):
            for ch in range(NCH):
                sy.wait_ge(cv.h, b_cv + ch + 1)
                ow.inc(sy.dma_start(d[dst][ch * 128:(ch + 1) * 128, :],
                                    s["fbuf"][:, FB_OUT:FB_OUT + E]), 16)
            sy.wait_ge(ow.h, ow.v)


def sel_ap(ctx, h):
    return bass.AP(ctx.s["consts_f"], 64 * 256 + 64 + h, [[256, 32], [0, 128]])


FA_STATS = 64
FA_STATST = 64 + 1024
FA_EXPT = 64 + 1792
FA_ARG = 64 + 2560
FA_Y = 64 + 2816
FA_STATE = 64 + 3072
AB_Q = 0
AB_G = 2048
AB_KF = 4096
AB_KT = 5120
AB_VT = 6144
AB_COS = 7168
AB_S2 = 8192
AB_TMP = 9216
AB_TILES = 10240
AC_SSQ, AC_T1, AC_T2, AC_RSTD = 56, 57, 58, 59


def emit_attn_a_stats(ctx, prev):
    nc = ctx.nc
    s, d, p = ctx.s, ctx.d, ctx.p
    fb = s["fbuf"]; ab = s["abuf"]
    cos_src = d["cosT_prev" if prev else "cosT_own"]
    s2_src = d["s2T_prev" if prev else "s2T_own"]
    ld = ctx.sem("as_ld"); b_ld = ld.v
    tr1 = ctx.sem("as_t1"); b_t1 = tr1.v
    tr2 = ctx.sem("as_t2"); b_t2 = tr2.v
    with nc.Block() as blk:
        @blk.gpsimd
        def _(g):
            ld.inc(g.dma_start(fb[0:32, 64:64 + E], d["aWin"][:, :]), 16)
            ld.inc(g.dma_start(ab[:, AB_COS:AB_COS + T], cos_src[:, :]), 16)
            ld.inc(g.dma_start(ab[:, AB_S2:AB_S2 + T], s2_src[:, :]), 16)
            g.wait_ge(ld.h, ld.v)
    with nc.Block() as blk:
        @blk.tensor
        def _(t):
            for k in range(NE):
                if k >= 2:
                    t.wait_ge(tr2.h, b_t2 + k - 1)
                tr1.inc(t.transpose((p["pC"] if k % 2 == 0 else p["pD"])[:, 0:32],
                                    fb[0:32, 64 + k * 128:64 + (k + 1) * 128],
                                    cident_f(ctx, 32)), 1)
        @blk.vector
        def _(v):
            for k in range(NE):
                v.wait_ge(tr1.h, b_t1 + k + 1)
                tr2.inc(v.tensor_copy(ab[:, AB_Q + k * 32:AB_Q + (k + 1) * 32],
                                      (p["pC"] if k % 2 == 0 else p["pD"])[:, 0:32]), 1)
    mm = ctx.sem("as_mm"); b_mm = mm.v
    ev = ctx.sem("as_ev"); b_ev = ev.v
    sfc = ctx.sem("as_sf"); b_sf = sfc.v
    with nc.Block() as blk:
        @blk.tensor
        def _(t):
            for nn in range(2):
                pp = p["pA"] if nn == 0 else p["pB"]
                for k in range(NE):
                    ins = t.matmul(pp[0:32, :], ab[:, AB_Q + k * 32:AB_Q + (k + 1) * 32],
                                   s["h_fm"][:, k * T + nn * 512:k * T + (nn + 1) * 512],
                                   start=(k == 0), stop=(k == NE - 1))
                    if k == NE - 1:
                        mm.inc(ins, 1)
        @blk.scalar
        def _(sc):
            ST0 = FA_STATS
            for nn in range(2):
                pp = p["pA"] if nn == 0 else p["pB"]
                sc.wait_ge(mm.h, b_mm + nn + 1)
                ev.inc(sc.add(fb[0:32, ST0 + nn * 512:ST0 + (nn + 1) * 512],
                              pp[0:32, :], s["small_f"][0:32, 0:1]), 1)
            sc.wait_ge(ev.h, b_ev + 2)
            sfc.inc(sc.activation(fb[0:32, ST0:ST0 + T], fb[0:32, ST0:ST0 + T],
                                  AF.Exp), 1)
            sc.wait_ge(sfc.h, b_sf + 1)
            sfc.inc(sc.activation(fb[0:32, ST0:ST0 + T], fb[0:32, ST0:ST0 + T],
                                  AF.Ln, bias=1.0), 1)
    vv = ctx.sem("as_vv"); b_vv = vv.v
    tt3 = ctx.sem("as_t3"); b_t3 = tt3.v
    tt4 = ctx.sem("as_t4"); b_t4 = tt4.v
    ex1 = ctx.sem("as_ex"); b_ex = ex1.v
    with nc.Block() as blk:
        ST0 = FA_STATS
        @blk.vector
        def _(v):
            v.wait_ge(sfc.h, b_sf + 2)
            vv.inc(v.tensor_scalar(fb[96:128, ST0:ST0 + T], fb[0:32, ST0:ST0 + T],
                                   s["small_f"][0:32, 2:3], None, ALU.mult), 1)
            v.wait_ge(vv.h, b_vv + 1)
            onesb = bass.AP(s["small_f"], 96 * 64 + 8, [[64, 32], [0, 128]])
            for ch in range(NCH):
                gch = fb[96:128, ST0 + ch * 128:ST0 + (ch + 1) * 128]
                vv.inc(v.tensor_tensor_scan(fb[64:96, ST0 + ch * 128:ST0 + (ch + 1) * 128],
                                            onesb, gch, 0.0, ALU.mult, ALU.add), 1)
                v.wait_ge(vv.h, b_vv + 3 * ch + 2)
                vv.inc(v.tensor_scalar(fb[32:64, ST0 + ch * 128:ST0 + (ch + 1) * 128],
                                       fb[64:96, ST0 + ch * 128:ST0 + (ch + 1) * 128],
                                       fb[64:96, ST0 + ch * 128 + 127:ST0 + ch * 128 + 128],
                                       None, ALU.subtract), 1)
                v.wait_ge(vv.h, b_vv + 3 * ch + 3)
                vv.inc(v.tensor_scalar(fb[32:64, ST0 + ch * 128:ST0 + (ch + 1) * 128],
                                       fb[32:64, ST0 + ch * 128:ST0 + (ch + 1) * 128],
                                       -1.0, None, ALU.mult), 1)
        @blk.vector
        def _(v2):
            for ch in range(NCH):
                v2.wait_ge(tt3.h, b_t3 + ch + 1)
                tt4.inc(v2.tensor_copy(fb[:, FA_STATST + ch * 96:FA_STATST + (ch + 1) * 96],
                                       (p["pC"] if ch % 2 == 0 else p["pD"])[:, 0:96]), 1)
        @blk.tensor
        def _(t):
            for ch in range(NCH):
                t.wait_ge(vv.h, b_vv + 1 + 3 * (ch + 1))
                if ch >= 2:
                    t.wait_ge(tt4.h, b_t4 + ch - 1)
                tt3.inc(t.transpose((p["pC"] if ch % 2 == 0 else p["pD"])[:, 0:96],
                                    fb[0:96, FA_STATS + ch * 128:FA_STATS + (ch + 1) * 128],
                                    cident_f(ctx, 96)), 1)
        @blk.scalar
        def _(sc):
            for ch in range(NCH):
                c0 = FA_STATST + ch * 96
                e0 = FA_EXPT + ch * 96
                sc.wait_ge(tt4.h, b_t4 + ch + 1)
                ex1.inc(sc.activation(fb[:, e0:e0 + 32], fb[:, c0 + 64:c0 + 96], AF.Exp), 1)
                ex1.inc(sc.activation(fb[:, e0 + 32:e0 + 64], fb[:, c0 + 32:c0 + 64],
                                      AF.Exp), 1)
        @blk.vector
        def _(v3):
            for ch in range(NCH):
                c0 = FA_STATST + ch * 96
                e0 = FA_EXPT + ch * 96
                v3.wait_ge(ex1.h, b_ex + 2 * (ch + 1))
                tt4.inc(v3.tensor_tensor(fb[:, e0 + 64:e0 + 96], fb[:, e0:e0 + 32],
                                         fb[:, e0 + 32:e0 + 64], ALU.mult), 1)
                v3.wait_ge(tt4.h, tt4.v)
                tt4.inc(v3.tensor_tensor(fb[:, e0 + 32:e0 + 64], fb[:, e0 + 32:e0 + 64],
                                         fb[:, c0:c0 + 32], ALU.mult), 1)
                v3.wait_ge(tt4.h, tt4.v)


KV_NO_ROPE = False
KV_NO_TRANS = False


def emit_attn_a_kv(ctx, grp):
    nc = ctx.nc
    s, d, p = ctx.s, ctx.d, ctx.p
    ab = s["abuf"]
    kw = ctx.sem("ak_w"); b_w = kw.v
    km = ctx.sem("ak_m"); b_m = km.v
    ke = ctx.sem("ak_e"); b_e = ke.v
    kr = ctx.sem("ak_r"); b_r = kr.v
    kt = ctx.sem("ak_t"); b_t = kt.v
    kc = ctx.sem("ak_c"); b_c = kc.v
    with nc.Block() as blk:
        @blk.sync
        def _(sy):
            for ww, wname in enumerate(["wk_t", "wv_t"]):
                kw.inc(sy.dma_start(
                    s["wsl"][:, ww * 8192:ww * 8192 + 4096].rearrange("p (k j) -> p k j", k=NE),
                    d[wname][:, grp * 128:(grp + 1) * 128].rearrange("(k p) j -> p k j", p=128)), 16)
        @blk.tensor
        def _(t):
            for ww in range(2):
                t.wait_ge(kw.h, b_w + 16 * (ww + 1))
                for nn in range(2):
                    m = ww * 2 + nn
                    pp = p["pA"] if nn == 0 else p["pB"]
                    if m >= 2:
                        t.wait_ge(ke.h, b_e + m - 1)
                    for k in range(NE):
                        ins = t.matmul(pp[:, :],
                                       s["wsl"][:, ww * 8192 + k * 128:ww * 8192 + (k + 1) * 128],
                                       s["h_fm"][:, k * T + nn * 512:k * T + (nn + 1) * 512],
                                       start=(k == 0), stop=(k == NE - 1))
                        if k == NE - 1:
                            km.inc(ins, 1)
            for ww in range(2):
                base = AB_KF if ww == 0 else AB_TMP
                t.wait_ge(kr.h, b_r + 1)
                if ww == 1:
                    t.wait_ge(ke.h, b_e + 4)
                for ch in range(NCH):
                    n = ww * 8 + ch
                    if n >= 2:
                        t.wait_ge(kc.h, b_c + n - 1)
                    if KV_NO_TRANS:
                        kt.inc(t.nop(), 1)
                        continue
                    kt.inc(t.transpose((p["pT1"] if n % 2 == 0 else p["pT2"])[:, ((n // 2) % 4) * 128:((n // 2) % 4) * 128 + 128],
                                       ab[:, base + ch * 128:base + (ch + 1) * 128],
                                       cident_bf(ctx)), 1)
        @blk.scalar
        def _(sc):
            for ww in range(2):
                base = AB_KF if ww == 0 else AB_TMP
                if ww == 1:
                    sc.wait_ge(kr.h, b_r + 1)
                for nn in range(2):
                    pp = p["pA"] if nn == 0 else p["pB"]
                    sc.wait_ge(km.h, b_m + 2 * ww + nn + 1)
                    ke.inc(sc.activation(ab[:, base + nn * 512:base + (nn + 1) * 512],
                                         pp[:, :], AF.Copy), 1)
        @blk.vector
        def _(v):
            v.wait_ge(ke.h, b_e + 2)
            if KV_NO_ROPE:
                kr.inc(v.tensor_copy(ab[:, AB_TMP:AB_TMP + T], ab[:, AB_KF:AB_KF + T]), 1)
            else:
                v.tensor_copy(ab[0:64, AB_TMP:AB_TMP + T], ab[64:128, AB_KF:AB_KF + T])
                v.tensor_copy(ab[64:128, AB_TMP:AB_TMP + T], ab[0:64, AB_KF:AB_KF + T])
                v.tensor_tensor(ab[:, AB_TMP:AB_TMP + T], ab[:, AB_TMP:AB_TMP + T],
                                ab[:, AB_S2:AB_S2 + T], ALU.mult)
                v.tensor_tensor(ab[:, AB_KF:AB_KF + T], ab[:, AB_KF:AB_KF + T],
                                ab[:, AB_COS:AB_COS + T], ALU.mult)
                kr.inc(v.tensor_tensor(ab[:, AB_KF:AB_KF + T], ab[:, AB_KF:AB_KF + T],
                                       ab[:, AB_TMP:AB_TMP + T], ALU.add), 1)
            for ww in range(2):
                dst = AB_KT if ww == 0 else AB_VT
                for ch in range(NCH):
                    n = ww * 8 + ch
                    v.wait_ge(kt.h, b_t + n + 1)
                    if KV_NO_TRANS:
                        kc.inc(v.memset(ab[:, dst + ch * 128:dst + (ch + 1) * 128], 0.0), 1)
                        continue
                    kc.inc(v.tensor_copy(ab[:, dst + ch * 128:dst + (ch + 1) * 128],
                                         (p["pT1"] if n % 2 == 0 else p["pT2"])[:, ((n // 2) % 4) * 128:((n // 2) % 4) * 128 + 128]), 1)


def emit_attn_a_carry_group(ctx, grp):
    nc = ctx.nc
    s, d, p = ctx.s, ctx.d, ctx.p
    ab = s["abuf"]; fb = s["fbuf"]
    kd = ctx.sem("ac_kd"); b_kd = kd.v
    cm = ctx.sem("ac_cm"); b_cm = cm.v
    su = ctx.sem("ac_su"); b_su = su.v
    cw = ctx.sem("ac_cw"); b_cw = cw.v
    state = fb[:, FA_STATE:FA_STATE + 128]
    with nc.Block() as blk:
        @blk.vector
        def _(v):
            for h4 in range(4):
                h = grp * 4 + h4
                if h4 >= 1:
                    v.wait_ge(cw.h, b_cw + 16 * h4)
                v.memset(state, 0.0)
                for ch in range(NCH):
                    n = h4 * NCH + ch
                    e0 = FA_EXPT + ch * 96
                    kd2 = ab[:, AB_TILES + 384 + (n % 2) * 128:AB_TILES + 384 + (n % 2) * 128 + 128]
                    if n >= 2:
                        v.wait_ge(cm.h, b_cm + n - 1)
                    kd.inc(v.tensor_scalar(kd2, ab[:, AB_KT + ch * 128:AB_KT + (ch + 1) * 128],
                                           fb[:, e0 + 32 + h:e0 + 33 + h], None, ALU.mult), 1)
                    v.wait_ge(cm.h, b_cm + n + 1)
                    su.inc(v.scalar_tensor_tensor(state, state,
                                                  fb[:, e0 + 64 + h:e0 + 65 + h],
                                                  p["pY" if n % 2 == 0 else "pZ"][:, 256:384],
                                                  ALU.mult, ALU.add), 1)
        @blk.tensor
        def _(t):
            for h4 in range(4):
                for ch in range(NCH):
                    n = h4 * NCH + ch
                    kd2 = ab[:, AB_TILES + 384 + (n % 2) * 128:AB_TILES + 384 + (n % 2) * 128 + 128]
                    t.wait_ge(kd.h, b_kd + n + 1)
                    if n >= 2:
                        t.wait_ge(su.h, b_su + n - 1)
                    cm.inc(t.matmul(p["pY" if n % 2 == 0 else "pZ"][:, 256:384], kd2,
                                    ab[:, AB_VT + ch * 128:AB_VT + (ch + 1) * 128],
                                    start=True, stop=True), 1)
        @blk.sync
        def _(sy):
            for h4 in range(4):
                h = grp * 4 + h4
                sy.wait_ge(su.h, b_su + NCH * (h4 + 1))
                cw.inc(sy.dma_start(d["carry_hbm"][h * 128:(h + 1) * 128, :], state), 16)
            sy.wait_ge(cw.h, cw.v)


def emit_attn_a_main_group(ctx, grp):
    nc = ctx.nc
    s, d, p = ctx.s, ctx.d, ctx.p
    ab = s["abuf"]; fb = s["fbuf"]
    state = fb[:, FA_STATE:FA_STATE + 128]
    aw = ctx.sem("am_aw"); b_aw = aw.v
    qp = ctx.sem("am_qp"); b_qp = qp.v
    qe = ctx.sem("am_qe"); b_qe = qe.v
    ar = ctx.sem("am_ar"); b_ar = ar.v
    gn = ctx.sem("am_gn"); b_gn = gn.v
    gw = ctx.sem("am_gw"); b_gw = gw.v
    gtr = ctx.sem("am_gt"); b_gt = gtr.v
    stg = ctx.sem("am_st"); b_st = stg.v
    agm = ctx.sem("am_ag"); b_ag = agm.v
    dex = ctx.sem("am_dx"); b_dx = dex.v
    sdr = ctx.sem("am_sd"); b_sd = sdr.v
    kdr = ctx.sem("am_kd"); b_kd2 = kdr.v
    pvb = ctx.sem("am_pv"); b_pv = pvb.v
    ymm = ctx.sem("am_ym"); b_ym = ymm.v
    csm = ctx.sem("am_cs"); b_cs = csm.v
    sup = ctx.sem("am_su"); b_sup = sup.v
    yv = ctx.sem("am_yv"); b_yv = yv.v
    rst = ctx.sem("am_rs"); b_rs = rst.v
    ygt = ctx.sem("am_yg"); b_yg = ygt.v
    ytr = ctx.sem("am_yt"); b_yt = ytr.v
    yge = ctx.sem("am_ye"); b_ye = yge.v
    ygw = ctx.sem("am_yw"); b_yw = ygw.v
    cld = ctx.sem("am_cl"); b_cl = cld.v

    def n_(h4, ch):
        return h4 * NCH + ch

    with nc.Block() as blk:
        @blk.sync
        def _(sy):
            def load_head(h4):
                h = grp * 4 + h4
                par = h4 % 2
                aw.inc(sy.dma_start(
                    s["wsl"][:, par * 4096:par * 4096 + 4096].rearrange("p (k j) -> p k j", k=NE),
                    d["wq_t"][:, h * 128:(h + 1) * 128].rearrange("(k p) j -> p k j", p=128)), 16)
                aw.inc(sy.dma_start(
                    s["wsl"][:, 8192 + par * 4096:8192 + par * 4096 + 4096].rearrange(
                        "p (k j) -> p k j", k=NE),
                    d["wg_t"][:, h * 128:(h + 1) * 128].rearrange("(k p) j -> p k j", p=128)), 16)
            load_head(0)
            load_head(1)
            for h4 in range(4):
                h = grp * 4 + h4
                par = h4 % 2
                sy.wait_ge(gn.h, b_gn + h4 + 1)
                gw.inc(sy.dma_start(d["b_hbm"][par * 128:(par + 1) * 128, 0:T],
                                    ab[:, AB_G + par * 1024:AB_G + par * 1024 + T]), 16)
                sy.wait_ge(gw.h, b_gw + 16 * (h4 + 1))
                for ch in range(NCH):
                    n = n_(h4, ch)
                    if n >= 2:
                        sy.wait_ge(ygt.h, b_yg + n - 1)
                    gtr.inc(sy.dma_start(
                        ab[:, AB_TILES + 768 + (n % 2) * 128:AB_TILES + 768 + (n % 2) * 128 + 128],
                        d["b_hbm"][par * 128:(par + 1) * 128, ch * 128:(ch + 1) * 128],
                        transpose=True), 16)
                if h4 + 2 < 4:
                    sy.wait_ge(qp.h, b_qp + 4 * (h4 + 1))
                    load_head(h4 + 2)
                sy.wait_ge(yge.h, b_ye + NCH * (h4 + 1))
                ygw.inc(sy.dma_start(d["yg_hbm"][h * 128:(h + 1) * 128, 0:T],
                                     ab[:, AB_G + par * 1024:AB_G + par * 1024 + T]), 16)
        @blk.gpsimd
        def _(g):
            for h4 in range(4):
                h = grp * 4 + h4
                if h4 >= 1:
                    g.wait_ge(sup.h, b_sup + NCH * h4)
                cld.inc(g.dma_start(state, d["carry_hbm"][h * 128:(h + 1) * 128, :]), 16)
        @blk.tensor
        def _(t):
            for h4 in range(4):
                par = h4 % 2
                qsl = s["wsl"][:, par * 4096:par * 4096 + 4096]
                gsl = s["wsl"][:, 8192 + par * 4096:8192 + par * 4096 + 4096]
                t.wait_ge(aw.h, b_aw + 32 * h4 + 16)
                for nn in range(2):
                    pp = p["pA"] if nn == 0 else p["pB"]
                    if h4 >= 1:
                        t.wait_ge(qe.h, b_qe + 4 * h4 - 4 + nn + 1 + 2)
                    for k in range(NE):
                        ins = t.matmul(pp[:, :], qsl[:, k * 128:(k + 1) * 128],
                                       s["h_fm"][:, k * T + nn * 512:k * T + (nn + 1) * 512],
                                       start=(k == 0), stop=(k == NE - 1))
                        if k == NE - 1:
                            qp.inc(ins, 1)
                t.wait_ge(aw.h, b_aw + 32 * h4 + 32)
                for nn in range(2):
                    pp = p["pA"] if nn == 0 else p["pB"]
                    t.wait_ge(qe.h, b_qe + 4 * h4 + nn + 1)
                    for k in range(NE):
                        ins = t.matmul(pp[:, :], gsl[:, k * 128:(k + 1) * 128],
                                       s["h_fm"][:, k * T + nn * 512:k * T + (nn + 1) * 512],
                                       start=(k == 0), stop=(k == NE - 1))
                        if k == NE - 1:
                            qp.inc(ins, 1)
                t.wait_ge(ar.h, b_ar + h4 + 1)
                for ch in range(NCH):
                    n = n_(h4, ch)
                    pc = p["pC"] if n % 2 == 0 else p["pD"]
                    py = p["pY"] if n % 2 == 0 else p["pZ"]
                    qt = ab[:, AB_Q + par * 1024 + ch * 128:AB_Q + par * 1024 + (ch + 1) * 128]
                    if n >= 2:
                        t.wait_ge(sdr.h, b_sd + n - 1)
                        t.wait_ge(agm.h, b_ag + n - 1)
                    t.matmul(pc[:, 0:128], ab[:, AB_KF + ch * 128:AB_KF + (ch + 1) * 128],
                             qt, start=True, stop=True)
                    stg.inc(t.matmul(pc[:, 128:256], sel_ap(ctx, grp * 4 + h4),
                                     fb[64:96, FA_STATS + ch * 128:FA_STATS + (ch + 1) * 128],
                                     start=True, stop=True), 1)
                    t.wait_ge(sdr.h, b_sd + n + 1)
                    t.wait_ge(pvb.h, b_pv + n + 1)
                    t.wait_ge(kdr.h, b_kd2 + n + 1)
                    if n >= 2:
                        t.wait_ge(yv.h, b_yv + 2 * n - 2)
                        t.wait_ge(sup.h, b_sup + n - 1)
                    sd = ab[:, AB_TILES + 128 + (n % 2) * 128:AB_TILES + 128 + (n % 2) * 128 + 128]
                    t.matmul(py[:, 0:128], sd, ab[:, AB_VT + ch * 128:AB_VT + (ch + 1) * 128],
                             start=True, stop=True)
                    ymm.inc(t.matmul(py[:, 128:256], qt,
                                     ab[:, AB_TILES + 640:AB_TILES + 640 + 128],
                                     start=True, stop=True), 1)
                    kd2 = ab[:, AB_TILES + 384 + (n % 2) * 128:AB_TILES + 384 + (n % 2) * 128 + 128]
                    csm.inc(t.matmul(py[:, 256:384], kd2,
                                     ab[:, AB_VT + ch * 128:AB_VT + (ch + 1) * 128],
                                     start=True, stop=True), 1)
                    t.wait_ge(ygt.h, b_yg + n + 1)
                    if n >= 2:
                        t.wait_ge(yge.h, b_ye + n - 1)
                    ytr.inc(t.transpose((p["pT1"] if n % 2 == 0 else p["pT2"])[:, ((n // 2) % 4) * 128:((n // 2) % 4) * 128 + 128],
                                        ab[:, AB_TMP + 512:AB_TMP + 640], cident_bf(ctx)), 1)
        @blk.scalar
        def _(sc):
            for h4 in range(4):
                par = h4 % 2
                for nn in range(2):
                    pp = p["pA"] if nn == 0 else p["pB"]
                    sc.wait_ge(qp.h, b_qp + 4 * h4 + nn + 1)
                    if h4 >= 2 and nn == 0:
                        sc.wait_ge(ymm.h, b_ym + NCH * (h4 - 1))
                    qe.inc(sc.activation(
                        ab[:, AB_Q + par * 1024 + nn * 512:AB_Q + par * 1024 + (nn + 1) * 512],
                        pp[:, :], AF.Copy), 1)
                for nn in range(2):
                    pp = p["pA"] if nn == 0 else p["pB"]
                    sc.wait_ge(qp.h, b_qp + 4 * h4 + 2 + nn + 1)
                    if h4 >= 2 and nn == 0:
                        sc.wait_ge(ygw.h, b_yw + 16 * (h4 - 1))
                    qe.inc(sc.activation(
                        ab[:, AB_G + par * 1024 + nn * 512:AB_G + par * 1024 + (nn + 1) * 512],
                        pp[:, :], AF.Silu), 1)
                for ch in range(NCH):
                    n = n_(h4, ch)
                    pc = p["pC"] if n % 2 == 0 else p["pD"]
                    sc.wait_ge(agm.h, b_ag + n + 1)
                    if n >= 1:
                        sc.wait_ge(sdr.h, b_sd + n)
                    dex.inc(sc.activation(ab[:, AB_TILES:AB_TILES + 128],
                                          fb[:, FA_ARG + (n % 2) * 128:FA_ARG + (n % 2) * 128 + 128],
                                          AF.Exp), 1)
                    sc.wait_ge(yv.h, b_yv + 2 * n + 2)
                    yf = fb[:, FA_Y + (n % 2) * 128:FA_Y + (n % 2) * 128 + 128]
                    sc.activation(ab[:, AB_TMP + 640:AB_TMP + 768], yf, AF.Square,
                                  accum_out=fb[:, AC_SSQ:AC_SSQ + 1])
                    rst.inc(sc.mul(fb[:, AC_T1:AC_T1 + 1], fb[:, AC_SSQ:AC_SSQ + 1],
                                   1.0 / 128.0), 1)
                    sc.wait_ge(rst.h, b_rs + 3 * n + 1)
                    rst.inc(sc.activation(fb[:, AC_T2:AC_T2 + 1], fb[:, AC_T1:AC_T1 + 1],
                                          AF.Ln, bias=s["small_f"][:, 9:10]), 1)
                    sc.wait_ge(rst.h, b_rs + 3 * n + 2)
                    if n >= 1:
                        sc.wait_ge(ygt.h, b_yg + n)
                    rst.inc(sc.activation(fb[:, AC_RSTD:AC_RSTD + 1], fb[:, AC_T2:AC_T2 + 1],
                                          AF.Exp, scale=s["small_f"][:, 10:11]), 1)
                    sc.wait_ge(ytr.h, b_yt + n + 1)
                    if ch == 0:
                        sc.wait_ge(gw.h, b_gw + 16 * (h4 + 1))
                    yge.inc(sc.activation(
                        ab[:, AB_G + par * 1024 + ch * 128:AB_G + par * 1024 + (ch + 1) * 128],
                        (p["pT1"] if n % 2 == 0 else p["pT2"])[:, ((n // 2) % 4) * 128:((n // 2) % 4) * 128 + 128], AF.Copy), 1)
        @blk.vector
        def _(v):
            for h4 in range(4):
                h = grp * 4 + h4
                par = h4 % 2
                q0 = AB_Q + par * 1024
                g0 = AB_G + par * 1024
                for hh in range(2):
                    o = hh * 512
                    v.wait_ge(qe.h, b_qe + 4 * h4 + hh + 1)
                    v.tensor_copy(ab[0:64, AB_TMP:AB_TMP + 512], ab[64:128, q0 + o:q0 + o + 512])
                    v.tensor_copy(ab[64:128, AB_TMP:AB_TMP + 512], ab[0:64, q0 + o:q0 + o + 512])
                    v.tensor_tensor(ab[:, AB_TMP:AB_TMP + 512], ab[:, AB_TMP:AB_TMP + 512],
                                    ab[:, AB_S2 + o:AB_S2 + o + 512], ALU.mult)
                    v.tensor_tensor(ab[:, q0 + o:q0 + o + 512], ab[:, q0 + o:q0 + o + 512],
                                    ab[:, AB_COS + o:AB_COS + o + 512], ALU.mult)
                    ins = v.tensor_tensor(ab[:, q0 + o:q0 + o + 512], ab[:, q0 + o:q0 + o + 512],
                                          ab[:, AB_TMP:AB_TMP + 512], ALU.add)
                    if hh == 1:
                        ar.inc(ins, 1)
                v.wait_ge(qe.h, b_qe + 4 * h4 + 4)
                gn.inc(v.tensor_scalar(ab[:, g0:g0 + T], ab[:, g0:g0 + T],
                                       s["small_f"][:, 3:4], None, ALU.mult), 1)
                for ch in range(NCH):
                    n = n_(h4, ch)
                    pc = p["pC"] if n % 2 == 0 else p["pD"]
                    py = p["pY"] if n % 2 == 0 else p["pZ"]
                    c0 = FA_STATST + ch * 96
                    e0 = FA_EXPT + ch * 96
                    v.wait_ge(stg.h, b_st + n + 1)
                    if n >= 2:
                        v.wait_ge(dex.h, b_dx + n - 1)
                    agm.inc(v.scalar_tensor_tensor(
                        fb[:, FA_ARG + (n % 2) * 128:FA_ARG + (n % 2) * 128 + 128],
                        pc[:, 128:256], fb[:, c0 + 64 + h:c0 + 64 + h + 1], cmask(ctx),
                        ALU.subtract, ALU.add), 1)
                    v.wait_ge(dex.h, b_dx + n + 1)
                    sd = ab[:, AB_TILES + 128 + (n % 2) * 128:AB_TILES + 128 + (n % 2) * 128 + 128]
                    sdr.inc(v.scalar_tensor_tensor(sd, pc[:, 0:128],
                                                   fb[:, c0 + h:c0 + h + 1],
                                                   ab[:, AB_TILES:AB_TILES + 128],
                                                   ALU.mult, ALU.mult), 1)
                    if n >= 2:
                        v.wait_ge(csm.h, b_cs + n - 1)
                    kd2 = ab[:, AB_TILES + 384 + (n % 2) * 128:AB_TILES + 384 + (n % 2) * 128 + 128]
                    kdr.inc(v.tensor_scalar(kd2, ab[:, AB_KT + ch * 128:AB_KT + (ch + 1) * 128],
                                            fb[:, e0 + 32 + h:e0 + 33 + h], None, ALU.mult), 1)
                    if ch == 0:
                        v.wait_ge(cld.h, b_cl + 16 * (h4 + 1))
                    pvb.inc(v.tensor_copy(ab[:, AB_TILES + 640:AB_TILES + 640 + 128], state), 1)
                    v.wait_ge(csm.h, b_cs + n + 1)
                    sup.inc(v.scalar_tensor_tensor(state, state, fb[:, e0 + 64 + h:e0 + 65 + h],
                                                   py[:, 256:384], ALU.mult, ALU.add), 1)
                    v.wait_ge(csm.h, b_cs + n + 1)
                    yf = fb[:, FA_Y + (n % 2) * 128:FA_Y + (n % 2) * 128 + 128]
                    yv.inc(v.tensor_scalar(yf, py[:, 128:256], fb[:, e0 + h:e0 + h + 1],
                                           None, ALU.mult), 1)
                    v.wait_ge(yv.h, b_yv + 2 * n + 1)
                    yv.inc(v.tensor_tensor(yf, py[:, 0:128], yf, ALU.add), 1)
                    v.wait_ge(rst.h, b_rs + 3 * n + 3)
                    v.wait_ge(gtr.h, b_gt + 16 * (n + 1))
                    if n >= 1:
                        v.wait_ge(ytr.h, b_yt + n)
                    gt_t = ab[:, AB_TILES + 768 + (n % 2) * 128:AB_TILES + 768 + (n % 2) * 128 + 128]
                    ygt.inc(v.scalar_tensor_tensor(ab[:, AB_TMP + 512:AB_TMP + 640], yf,
                                                   fb[:, AC_RSTD:AC_RSTD + 1], gt_t,
                                                   ALU.mult, ALU.mult), 1)
    return


def emit_attn_a_wo(ctx):
    nc = ctx.nc
    s, d, p = ctx.s, ctx.d, ctx.p
    ab = s["abuf"]
    wl = ctx.sem("aw_wl"); b_wl = wl.v
    yl = ctx.sem("aw_yl"); b_yl = yl.v
    mmv = ctx.sem("aw_mm"); b_mm = mmv.v
    vad = ctx.sem("aw_va"); b_va = vad.v
    with nc.Block() as blk:
        @blk.sync
        def _(sy):
            for ot in range(16):
                par = ot % 2
                if ot >= 2:
                    sy.wait_ge(mmv.h, b_mm + (ot - 1) * NCH)
                wl.inc(sy.dma_start(
                    s["wsl"][:, par * 8192:par * 8192 + 8192].rearrange("p (k j) -> p k j", k=NE),
                    d["wo_t"][:, ot * 256:(ot + 1) * 256].rearrange("(k p) j -> p k j", p=128)), 16)
        @blk.gpsimd
        def _(g):
            for ot in range(16):
                for ch in range(NCH):
                    m = ot * NCH + ch
                    if m >= 2:
                        g.wait_ge(mmv.h, b_mm + m - 1)
                    yl.inc(g.dma_start(
                        ab[:, (m % 2) * 4096:(m % 2) * 4096 + 4096].rearrange(
                            "p (k j) -> p k j", k=NE),
                        d["yg_hbm"][:, ch * 128:(ch + 1) * 128].rearrange(
                            "(k p) j -> p k j", p=128)), 16)
        @blk.tensor
        def _(t):
            for ot in range(16):
                par = ot % 2
                t.wait_ge(wl.h, b_wl + 16 * (ot + 1))
                for ch in range(NCH):
                    m = ot * NCH + ch
                    pp = p["pA"] if m % 2 == 0 else p["pB"]
                    t.wait_ge(yl.h, b_yl + 16 * (m + 1))
                    if m >= 2:
                        t.wait_ge(vad.h, b_va + m - 1)
                    for k in range(NE):
                        ins = t.matmul(pp[:, 0:256],
                                       ab[:, (m % 2) * 4096 + k * 128:(m % 2) * 4096 + (k + 1) * 128],
                                       s["wsl"][:, par * 8192 + k * 256:par * 8192 + (k + 1) * 256],
                                       start=(k == 0), stop=(k == NE - 1))
                        if k == NE - 1:
                            mmv.inc(ins, 1)
        @blk.vector
        def _(v):
            for ot in range(16):
                for ch in range(NCH):
                    m = ot * NCH + ch
                    pp = p["pA"] if m % 2 == 0 else p["pB"]
                    v.wait_ge(mmv.h, b_mm + m + 1)
                    xsl = s["x_sb"][:, ch * E + ot * 256:ch * E + (ot + 1) * 256]
                    vad.inc(v.tensor_tensor(xsl, xsl, pp[:, 0:256], ALU.add), 1)


def emit_attn_a(ctx):
    emit_load_x(ctx, "xprev")
    emit_ln(ctx)
    emit_attn_a_stats(ctx, prev=True)
    for grp in range(KVH):
        emit_attn_a_kv(ctx, grp)
        emit_attn_a_carry_group(ctx, grp)
    emit_load_x(ctx, "x0")
    emit_ln(ctx)
    emit_attn_a_stats(ctx, prev=False)
    for grp in range(KVH):
        emit_attn_a_kv(ctx, grp)
        emit_attn_a_main_group(ctx, grp)
    emit_attn_a_wo(ctx)


def emit_flush(ctx):
    nc = ctx.nc
    with nc.Block() as blk:
        @blk.sync
        def _(sy):
            for sm in getattr(ctx, "all_sems", []):
                if sm.v > 0:
                    sy.wait_ge(sm.h, sm.v)


XB_Q = 0
XB_K = 8192
XB_V = 16384
XB_S = 24576
BB_UNN = 0
BB_OUT = 2048
BB_OB = 4096
BB_BROW = 8192
BB_ONES = 11136
INV_SQRT_D = 1.0 / float(np.sqrt(128.0))


def emit_scratch_attb(ctx):
    nc = ctx.nc
    s, d, p = ctx.s, ctx.d, ctx.p
    ab = s["abuf"]
    ld = ctx.sem("bs_ld"); b_ld = ld.v
    vv = ctx.sem("bs_vv"); b_vv = vv.v
    wr = ctx.sem("bs_wr"); b_wr = wr.v
    bq2 = d["bWq"][:, :, :].rearrange("a b c -> a (b c)")
    bk2 = d["bWk"][:, :, :].rearrange("a b c -> a (b c)")
    bv2 = d["bWv"][:, :, :].rearrange("a b c -> a (b c)")
    with nc.Block() as blk:
        @blk.gpsimd
        def _(g):
            for i in range(32):
                if i >= 1:
                    g.wait_ge(vv.h, b_vv + 4 * i)
                ld.inc(g.dma_start(ab[:, 0:4096], bq2[i * 128:(i + 1) * 128, :]), 16)
                g.wait_ge(vv.h, b_vv + 4 * i + 3)
                ld.inc(g.dma_start(ab[:, 0:1024], bk2[i * 128:(i + 1) * 128, :]), 16)
                ld.inc(g.dma_start(ab[:, 1024:2048], bv2[i * 128:(i + 1) * 128, :]), 16)
        @blk.vector
        def _(v):
            for i in range(32):
                v.wait_ge(ld.h, b_ld + 48 * i + 16)
                vv.inc(v.tensor_tensor(ab[:, 4096:6144],
                                bass.AP(ab, 0, [[ABUF_W, 128], [2, 2048]]),
                                bass.AP(ab, 1, [[ABUF_W, 128], [2, 2048]]), ALU.add), 1)
                v.wait_ge(vv.h, b_vv + 4 * i + 1)
                vv.inc(v.tensor_tensor(ab[:, 6144:7168],
                                bass.AP(ab, 4096, [[ABUF_W, 128], [2, 1024]]),
                                bass.AP(ab, 4097, [[ABUF_W, 128], [2, 1024]]), ALU.add), 1)
                v.wait_ge(vv.h, b_vv + 4 * i + 2)
                if i >= 1:
                    v.wait_ge(wr.h, b_wr + 48 * (i - 1) + 16)
                vv.inc(v.tensor_scalar(ab[:, 7168:8192],
                                       bass.AP(ab, 6144, [[ABUF_W, 128], [8, 128], [1, 8]])
                                       if False else
                                       bass.AP(ab, 6144, [[ABUF_W, 128], [1, 8], [8, 128]]),
                                       INV_SQRT_D, None, ALU.mult), 1)
                v.wait_ge(ld.h, b_ld + 48 * i + 48)
                if i >= 1:
                    v.wait_ge(wr.h, b_wr + 48 * (i - 1) + 48)
                v.tensor_copy(ab[:, 8192:9216],
                              bass.AP(ab, 0, [[ABUF_W, 128], [1, 8], [8, 128]]))
                vv.inc(v.tensor_copy(ab[:, 9216:10240],
                                     bass.AP(ab, 1024, [[ABUF_W, 128], [1, 8], [8, 128]])), 1)
        @blk.sync
        def _(sy):
            for i in range(32):
                sy.wait_ge(vv.h, b_vv + 4 * i + 3)
                wr.inc(sy.dma_start(d["bqsum_nat"][i * 128:(i + 1) * 128, :],
                                    ab[:, 7168:8192]), 16)
                sy.wait_ge(vv.h, b_vv + 4 * i + 4)
                wr.inc(sy.dma_start(d["bk8_nat"][i * 128:(i + 1) * 128, :],
                                    ab[:, 8192:9216]), 16)
                wr.inc(sy.dma_start(d["bv8_nat"][i * 128:(i + 1) * 128, :],
                                    ab[:, 9216:10240]), 16)
            sy.wait_ge(wr.h, wr.v)
    bl = ctx.sem("bs_bl"); b_bl = bl.v
    bv2s = ctx.sem("bs_b2"); b_b2 = bv2s.v
    with nc.Block() as blk:
        @blk.gpsimd
        def _(g):
            fbr = s["fbuf"]
            bl.inc(g.dma_start(fbr[0:1, 64:64 + 4096],
                               d["b_bq"][:, :].rearrange("a b -> (a b)").rearrange(
                                   "(x y) -> x y", x=1)), 16)
            g.memset(ab[0:1, BB_ONES:BB_ONES + 128], 1.0)
        @blk.vector
        def _(v):
            fbr = s["fbuf"]
            v.wait_ge(bl.h, b_bl + 16)
            bv2s.inc(v.tensor_tensor(fbr[0:1, 0:2048],
                            bass.AP(fbr, 64, [[FBUF_W, 1], [2, 2048]]),
                            bass.AP(fbr, 65, [[FBUF_W, 1], [2, 2048]]), ALU.add), 1)
            v.wait_ge(bv2s.h, b_b2 + 1)
            bv2s.inc(v.tensor_tensor(fbr[0:1, 2048:3072],
                            bass.AP(fbr, 0, [[FBUF_W, 1], [2, 1024]]),
                            bass.AP(fbr, 1, [[FBUF_W, 1], [2, 1024]]), ALU.add), 1)
            v.wait_ge(bv2s.h, b_b2 + 2)
            bv2s.inc(v.tensor_scalar(ab[0:1, BB_BROW:BB_BROW + 1024],
                                     bass.AP(fbr, 2048, [[FBUF_W, 1], [1, 8], [8, 128]]),
                                     INV_SQRT_D, None, ALU.mult), 1)
        @blk.gpsimd
        def _(g2):
            fbr = s["fbuf"]
            g2.wait_ge(bv2s.h, b_b2 + 3)
            bl.inc(g2.dma_start(fbr[0:1, 64:64 + 1024],
                                d["b_bk"][:, :].rearrange("a b -> (a b)").rearrange(
                                    "(x y) -> x y", x=1)), 16)
            bl.inc(g2.dma_start(fbr[0:1, 1088:1088 + 1024],
                                d["b_bv"][:, :].rearrange("a b -> (a b)").rearrange(
                                    "(x y) -> x y", x=1)), 16)
        @blk.vector
        def _(v2):
            fbr = s["fbuf"]
            v2.wait_ge(bl.h, b_bl + 48)
            bv2s.inc(v2.tensor_copy(ab[0:1, BB_BROW + 1024:BB_BROW + 2048],
                                    bass.AP(fbr, 64, [[FBUF_W, 1], [1, 8], [8, 128]])), 1)
            bv2s.inc(v2.tensor_copy(ab[0:1, BB_BROW + 2048:BB_BROW + 3072],
                                    bass.AP(fbr, 1088, [[FBUF_W, 1], [1, 8], [8, 128]])), 1)
        @blk.sync
        def _(sy3):
            sy3.wait_ge(bv2s.h, bv2s.v)
            bl2 = ctx.sem("bs_bw")
            bl2.inc(sy3.dma_start(d["brow_hbm"][:, :], ab[0:1, BB_BROW:BB_BROW + 3072]), 16)
            sy3.wait_ge(bl2.h, bl2.v)


BB_STAGE = 99

BX_QS = 0
BX_KS = 8192
BX_V8 = 16384
BX_BROW = 25600
BA_OST = 0
BA_UNN = 4096
BA_OB = 5120
BA_Z = 7168
BA_ONES = 8192


def emit_attn_b(ctx):
    nc = ctx.nc
    s, d, p = ctx.s, ctx.d, ctx.p
    ab = s["abuf"]; xb = s["x_sb"]; fb = s["fbuf"]
    XROW = NCH * E
    FB_RZ = 1152
    sp = ctx.sem("bb_sp")
    with nc.Block() as blk:
        @blk.sync
        def _(sy):
            for ch in range(NCH):
                sp.inc(sy.dma_start(d["b_hbm"][ch * 128:(ch + 1) * 128, :],
                                    xb[:, ch * E:(ch + 1) * E]), 16)
            sy.wait_ge(sp.h, sp.v)
    zi = ctx.sem("bb_zi")
    with nc.Block() as blk:
        @blk.sync
        def _(sy):
            zi.inc(sy.dma_start(xb[0:1, BX_BROW:BX_BROW + 3072], d["brow_hbm"][:, :]), 16)
            sy.wait_ge(zi.h, zi.v)
        @blk.vector
        def _(v):
            v.memset(xb[:, BX_QS:BX_QS + 16384], 0.0)
            v.memset(xb[:, BX_V8 + 8 * 1024:BX_V8 + 9 * 1024], 1.0)
            v.memset(ab[0:1, BA_ONES:BA_ONES + 512], 1.0)
    if BB_STAGE < 1:
        return
    for wi, (wnat, reg, brow_off) in enumerate([
            ("bqsum_nat", 0, BX_BROW), ("bk8_nat", 1024, BX_BROW + 1024)]):
        wl = ctx.sem("bb_wl"); b_wl = wl.v
        mm = ctx.sem("bb_mm"); b_mm = mm.v
        evq = ctx.sem("bb_ev"); b_ev = evq.v
        pw = ctx.sem("bb_pw"); b_pw = pw.v
        with nc.Block() as blk:
            @blk.sync
            def _(sy):
                def loadw(oq):
                    wl.inc(sy.dma_start(
                        s["wsl"][:, (oq % 2) * 8192:(oq % 2) * 8192 + 8192].rearrange(
                            "p (k j) -> p k j", k=NE),
                        d[wnat][:, oq * 256:(oq + 1) * 256].rearrange(
                            "(k p) j -> p k j", p=128)), 16)
                loadw(0)
                loadw(1)
                for oq in range(4):
                    for ch in range(NCH):
                        n = oq * NCH + ch
                        sy.wait_ge(evq.h, b_ev + n + 1)
                        pw.inc(sy.dma_start(
                            d["h_hbm"][ch * 128:(ch + 1) * 128, reg + oq * 256:reg + (oq + 1) * 256],
                            ab[0:128, BA_UNN + (n % 2) * 256:BA_UNN + (n % 2) * 256 + 256]), 16)
                    if oq + 2 < 4:
                        sy.wait_ge(mm.h, b_mm + (oq + 1) * NCH)
                        loadw(oq + 2)
            @blk.tensor
            def _(t):
                for oq in range(4):
                    t.wait_ge(wl.h, b_wl + 16 * (oq + 1))
                    for ch in range(NCH):
                        n = oq * NCH + ch
                        pp = p["pA"] if n % 2 == 0 else p["pB"]
                        if n >= 2:
                            t.wait_ge(evq.h, b_ev + n - 1)
                        for k in range(NE):
                            t.matmul(pp[:, 0:256],
                                     s["h_fm"][:, k * T + ch * 128:k * T + (ch + 1) * 128],
                                     s["wsl"][:, (oq % 2) * 8192 + k * 256:(oq % 2) * 8192 + (k + 1) * 256],
                                     start=(k == 0), stop=False)
                        mm.inc(t.matmul(pp[:, 0:256], ab[0:1, BA_ONES:BA_ONES + 128],
                                        xb[0:1, brow_off + oq * 256:brow_off + (oq + 1) * 256],
                                        start=False, stop=True), 1)
            @blk.scalar
            def _(sc):
                for oq in range(4):
                    for ch in range(NCH):
                        n = oq * NCH + ch
                        pp = p["pA"] if n % 2 == 0 else p["pB"]
                        sc.wait_ge(mm.h, b_mm + n + 1)
                        if n >= 2:
                            sc.wait_ge(pw.h, b_pw + 16 * (n - 1))
                        evq.inc(sc.activation(
                            ab[0:128, BA_UNN + (n % 2) * 256:BA_UNN + (n % 2) * 256 + 256],
                            pp[:, 0:256], AF.Copy), 1)
    wl2 = ctx.sem("bb_w2"); b_w2 = wl2.v
    mm2 = ctx.sem("bb_m2"); b_m2 = mm2.v
    ev2 = ctx.sem("bb_e2"); b_e2 = ev2.v
    with nc.Block() as blk:
        @blk.sync
        def _(sy):
            for g8 in range(8):
                if g8 >= 2:
                    sy.wait_ge(mm2.h, b_m2 + 2 * (g8 - 1))
                wl2.inc(sy.dma_start(
                    s["wsl"][:, (g8 % 2) * 8192:(g8 % 2) * 8192 + 4096].rearrange(
                        "p (k j) -> p k j", k=NE),
                    d["bv8_nat"][:, g8 * 128:(g8 + 1) * 128].rearrange(
                        "(k p) j -> p k j", p=128)), 16)
        @blk.tensor
        def _(t):
            for g8 in range(8):
                t.wait_ge(wl2.h, b_w2 + 16 * (g8 + 1))
                for nn in range(2):
                    n = g8 * 2 + nn
                    pp = p["pA"] if n % 2 == 0 else p["pB"]
                    if n >= 2:
                        t.wait_ge(ev2.h, b_e2 + n - 1)
                    for k in range(NE):
                        t.matmul(pp[:, :],
                                 s["wsl"][:, (g8 % 2) * 8192 + k * 128:(g8 % 2) * 8192 + (k + 1) * 128],
                                 s["h_fm"][:, k * T + nn * 512:k * T + (nn + 1) * 512],
                                 start=(k == 0), stop=False)
                    mm2.inc(t.matmul(pp[:, :],
                                     xb[0:1, BX_BROW + 2048 + g8 * 128:BX_BROW + 2048 + (g8 + 1) * 128],
                                     ab[0:1, BA_ONES:BA_ONES + 512],
                                     start=False, stop=True), 1)
        @blk.scalar
        def _(sc):
            for g8 in range(8):
                for nn in range(2):
                    n = g8 * 2 + nn
                    pp = p["pA"] if n % 2 == 0 else p["pB"]
                    sc.wait_ge(mm2.h, b_m2 + n + 1)
                    ev2.inc(sc.activation(
                        xb[:, BX_V8 + g8 * 1024 + nn * 512:BX_V8 + g8 * 1024 + (nn + 1) * 512],
                        pp[:, :], AF.Copy), 1)
    if BB_STAGE < 2:
        return
    gq = ctx.sem("bb_gq"); b_gq = gq.v
    sm2 = ctx.sem("bb_s2"); b_s2 = sm2.v
    ex = ctx.sem("bb_ex"); b_ex = ex.v
    om = ctx.sem("bb_om"); b_om = om.v
    oe = ctx.sem("bb_oe"); b_oe = oe.v
    osw = ctx.sem("bb_os"); b_os = osw.v
    ug = ctx.sem("bb_ug"); b_ug = ug.v
    ow2 = ctx.sem("bb_o2"); b_o2 = ow2.v
    NHC = T // 64
    with nc.Block() as blk:
        @blk.gpsimd
        def _(g):
            for hc in range(NHC):
                ch = hc // 2
                hoff = (hc % 2) * 64
                if hc >= 1:
                    g.wait_ge(sm2.h, b_s2 + 64 * hc)
                for reg, dstb in [(0, BX_QS), (1024, BX_KS)]:
                    src = bass.AP(d["h_hbm"], (ch * 128 + hoff) * 4096 + reg,
                                  [[128, 8], [4096, 64], [1, 128]])
                    dst = bass.AP(xb, dstb, [[XROW, 8], [128, 64], [1, 128]])
                    gq.inc(g.dma_start(dst, src), 16)
        @blk.tensor
        def _(t):
            for hc in range(NHC):
                t.wait_ge(gq.h, b_gq + 32 * (hc + 1))
                for tt in range(64):
                    n = hc * 64 + tt
                    grpj = n % 4
                    pc = p["pC"] if (n // 4) % 2 == 0 else p["pD"]
                    po = p["pY"] if (n // 4) % 2 == 0 else p["pZ"]
                    if grpj == 0 and n >= 8:
                        t.wait_ge(ex.h, b_ex + (n // 4) - 1)
                    sm2.inc(t.matmul(
                        pc[:, grpj * 128:(grpj + 1) * 128],
                        xb[0:32, BX_KS + tt * 128:BX_KS + (tt + 1) * 128],
                        xb[0:32, BX_QS + tt * 128:BX_QS + (tt + 1) * 128],
                        start=True, stop=True), 1)
                    if n >= 4:
                        n2 = n - 4
                        hc2 = n2 // 64
                        tt2 = n2 % 64
                        tglob2 = (hc2 // 2) * 128 + (hc2 % 2) * 64 + tt2
                        grpj2 = n2 % 4
                        po2 = p["pY"] if (n2 // 4) % 2 == 0 else p["pZ"]
                        t.wait_ge(ex.h, b_ex + (n2 // 4) + 1)
                        if grpj2 == 0 and n2 >= 8:
                            t.wait_ge(oe.h, b_oe + (n2 // 4) - 1)
                        om.inc(t.matmul(
                            po2[0:9, grpj2 * 128:(grpj2 + 1) * 128],
                            bass.AP(xb, BX_V8 + tglob2 % 1024,
                                    [[XROW, 128], [1024, 9]]),
                            ab[:, BA_UNN + ((n2 // 4) % 2) * 512 + grpj2 * 128:
                               BA_UNN + ((n2 // 4) % 2) * 512 + (grpj2 + 1) * 128],
                            start=True, stop=True), 1)
            for n2 in range(NHC * 64 - 4, NHC * 64):
                tt2 = n2 % 64
                hc2 = n2 // 64
                tglob2 = (hc2 // 2) * 128 + (hc2 % 2) * 64 + tt2
                grpj2 = n2 % 4
                po2 = p["pY"] if (n2 // 4) % 2 == 0 else p["pZ"]
                t.wait_ge(ex.h, b_ex + (n2 // 4) + 1)
                om.inc(t.matmul(
                    po2[0:9, grpj2 * 128:(grpj2 + 1) * 128],
                    bass.AP(xb, BX_V8 + tglob2 % 1024, [[XROW, 128], [1024, 9]]),
                    ab[:, BA_UNN + ((n2 // 4) % 2) * 512 + grpj2 * 128:
                       BA_UNN + ((n2 // 4) % 2) * 512 + (grpj2 + 1) * 128],
                    start=True, stop=True), 1)
        @blk.scalar
        def _(sc):
            NG = NHC * 16
            for gidx in range(NG):
                pc = p["pC"] if gidx % 2 == 0 else p["pD"]
                sc.wait_ge(sm2.h, b_s2 + 4 * (gidx + 1))
                if gidx >= 2:
                    sc.wait_ge(om.h, b_om + 4 * (gidx - 1))
                ex.inc(sc.activation(ab[:, BA_UNN + (gidx % 2) * 512:BA_UNN + (gidx % 2) * 512 + 512],
                                     pc[:, :], AF.Exp), 1)
                if gidx >= 1:
                    g2 = gidx - 1
                    po2 = p["pY"] if g2 % 2 == 0 else p["pZ"]
                    sc.wait_ge(om.h, b_om + 4 * (g2 + 1))
                    if g2 >= 8:
                        sc.wait_ge(osw.h, b_os + 16 * (g2 // 8))
                    oe.inc(sc.activation(ab[0:9, BA_OST + (g2 % 8) * 512:BA_OST + (g2 % 8) * 512 + 512],
                                         po2[0:9, :], AF.Copy), 1)
            for g2 in range(NG - 1, NG):
                po2 = p["pY"] if g2 % 2 == 0 else p["pZ"]
                sc.wait_ge(om.h, b_om + 4 * (g2 + 1))
                oe.inc(sc.activation(ab[0:9, BA_OST + (g2 % 8) * 512:BA_OST + (g2 % 8) * 512 + 512],
                                     po2[0:9, :], AF.Copy), 1)
        @blk.sync
        def _(sy):
            NQ = T // 32
            for qq in range(NQ):
                sy.wait_ge(oe.h, b_oe + 8 * (qq + 1))
                osw.inc(sy.dma_start(
                    d["st_hbm"][0:9, (qq % 2) * 4096:(qq % 2) * 4096 + 4096],
                    ab[0:9, BA_OST:BA_OST + 4096]), 16)
                ch = qq // 4
                qoff = (qq % 4) * 32
                sy.wait_ge(osw.h, osw.v)
                if qoff == 0 and ch >= 2:
                    sy.wait_ge(ow2.h, b_o2 + 16 * (ch - 1))
                srco = bass.AP(d["st_hbm"], (qq % 2) * 4096, [[128, 32], [8192, 8], [1, 128]])
                dsto = bass.AP(ab, qoff * ABUF_W + BA_OB + (ch % 2) * 1024,
                               [[ABUF_W, 32], [128, 8], [1, 128]])
                ug.inc(sy.dma_start(dsto, srco), 16)
                srcz = bass.AP(d["st_hbm"], 8 * 8192 + (qq % 2) * 4096, [[128, 32], [1, 128]])
                dstz = bass.AP(ab, qoff * ABUF_W + BA_Z + ch * 128, [[ABUF_W, 32], [1, 128]])
                ug.inc(sy.dma_start(dstz, srcz), 16)
                if qoff == 96:
                    sy.wait_ge(ug.h, b_ug + 32 * (qq + 1))
                    ow2.inc(sy.dma_start(d["yg_hbm"][ch * 128:(ch + 1) * 128, 0:1024],
                                         ab[:, BA_OB + (ch % 2) * 1024:BA_OB + (ch % 2) * 1024 + 1024]), 16)
            sy.wait_ge(ow2.h, ow2.v)
    if BB_STAGE < 3:
        return
    xl = ctx.sem("bb_xl"); b_xl = xl.v
    ol = ctx.sem("bb_ol"); b_ol = ol.v
    fz = ctx.sem("bb_fz"); b_fz = fz.v
    with nc.Block() as blk:
        @blk.gpsimd
        def _(g):
            for ch in range(NCH):
                if ch >= 2:
                    g.wait_ge(fz.h, b_fz + 3 * (ch - 1))
                xl.inc(g.dma_start(xb[:, ch * E:(ch + 1) * E],
                                   d["b_hbm"][ch * 128:(ch + 1) * 128, :]), 16)
                ol.inc(g.dma_start(ab[:, BA_OB + (ch % 2) * 1024:BA_OB + (ch % 2) * 1024 + 1024],
                                   d["yg_hbm"][ch * 128:(ch + 1) * 128, 0:1024]), 16)
        @blk.vector
        def _(v):
            for ch in range(NCH):
                v.wait_ge(xl.h, b_xl + 16 * (ch + 1))
                v.wait_ge(ol.h, b_ol + 16 * (ch + 1))
                fz.inc(v.reciprocal(fb[:, FB_RZ:FB_RZ + 128],
                                    ab[:, BA_Z + ch * 128:BA_Z + (ch + 1) * 128]), 1)
                v.wait_ge(fz.h, b_fz + 3 * ch + 1)
                fz.inc(v.tensor_tensor(ab[:, BA_OB + (ch % 2) * 1024:BA_OB + (ch % 2) * 1024 + 1024],
                                       ab[:, BA_OB + (ch % 2) * 1024:BA_OB + (ch % 2) * 1024 + 1024],
                                       bass.AP(fb, FB_RZ, [[FBUF_W, 128], [0, 8], [1, 128]]),
                                       ALU.mult), 1)
                v.wait_ge(fz.h, b_fz + 3 * ch + 2)
                fz.inc(v.tensor_tensor(
                    xb[:, ch * E:(ch + 1) * E],
                    xb[:, ch * E:(ch + 1) * E],
                    bass.AP(ab, BA_OB + (ch % 2) * 1024,
                            [[ABUF_W, 128], [1, 128], [128, 8], [0, 4]]),
                    ALU.add), 1)


from concourse.bass_utils import run_bass_kernel_spmd


def build_program():
    nc = bass.Bass("TRN2", target_bir_lowering=False)
    ctx = Ctx(nc)
    declare_io(ctx)
    declare_sbuf(ctx)
    emit_load_consts(ctx)
    for nm in ["wq", "wk", "wv", "wo", "wg"]:
        emit_scratch_cast(ctx, ctx.d[W_SRC[nm]][:, :], nm + "_nat", W_NAT_ROWS[nm])
        emit_scratch_transpose(ctx, nm + "_nat", nm + "_t", W_NAT_ROWS[nm])
    emit_scratch_ffn(ctx)
    emit_scratch_transpose(ctx, "fw_nat", "fw_t", 4096)
    emit_scratch_attb(ctx)
    emit_attn_a(ctx)
    emit_ln(ctx)
    emit_ffn_main(ctx)
    emit_ln(ctx)
    emit_attn_b(ctx)
    emit_ln(ctx)
    emit_ffn_main(ctx)
    emit_store_out(ctx)
    emit_flush(ctx)
    return nc


def kernel(**inputs):
    X = np.ascontiguousarray(np.asarray(inputs["L0_attn_a_hidden_states"], np.float32))
    base = {
        "aWq": inputs["L0_attn_a_Wq"], "aWk": inputs["L0_attn_a_Wk"],
        "aWv": inputs["L0_attn_a_Wv"], "aWo": inputs["L0_attn_a_Wo"],
        "aWg": inputs["L0_attn_a_Wg"], "aWin": inputs["L0_attn_a_Win"],
        "a_dtb": inputs["L0_attn_a_dt_bias"], "a_Alog": inputs["L0_attn_a_A_log_bias"],
        "a_gnorm": inputs["L0_attn_a_gnorm_w"],
        "bWq": inputs["L0_attn_b_Wq"], "b_bq": inputs["L0_attn_b_bq"],
        "bWk": inputs["L0_attn_b_Wk"], "b_bk": inputs["L0_attn_b_bk"],
        "bWv": inputs["L0_attn_b_Wv"], "b_bv": inputs["L0_attn_b_bv"],
        "fW": inputs["L0_ffn_W"], "f_lA": inputs["L0_ffn_lora_A"],
        "f_lB": inputs["L0_ffn_lora_B"], "f_mag": inputs["L0_ffn_magnitude"],
    }
    base = {k: np.ascontiguousarray(np.asarray(v, np.float32)) for k, v in base.items()}
    base.update(host_consts())
    cosf, s2f = rope_host_tables()
    zprev = np.zeros((T, E), np.float32)
    cosP = np.ascontiguousarray(cosf[0:T].T)
    s2P = np.ascontiguousarray(s2f[0:T].T)
    nc = build_program()
    in_maps = []
    for c in range(8):
        b, half = c // 2, c % 2
        im = dict(base)
        im["x0"] = np.ascontiguousarray(X[b, half * T:(half + 1) * T])
        im["xprev"] = zprev if half == 0 else np.ascontiguousarray(X[b, 0:T])
        im["cosT_own"] = np.ascontiguousarray(cosf[half * T:(half + 1) * T].T)
        im["s2T_own"] = np.ascontiguousarray(s2f[half * T:(half + 1) * T].T)
        im["cosT_prev"] = cosP
        im["s2T_prev"] = s2P
        in_maps.append(im)
    res = run_bass_kernel_spmd(nc, in_maps, core_ids=list(range(8)))
    out = np.empty((4, 2048, 4096), np.float32)
    for c in range(8):
        b, half = c // 2, c % 2
        out[b, half * T:(half + 1) * T] = np.asarray(res.results[c]["out"], np.float32)
    return out
